# version 01
# baseline (speedup 1.0000x reference)
import contextlib
import numpy as np
import ml_dtypes
import concourse.bass as bass
import concourse.mybir as mybir
from concourse.ap import AP
from concourse.bass_utils import run_bass_kernel_spmd

F32 = mybir.dt.float32
BF16 = mybir.dt.bfloat16
I32 = mybir.dt.int32
AF = mybir.ActivationFunctionType
ALU = mybir.AluOpType
AX = mybir.AxisListType

T = 2048
D = 4096
NH = 16
OC = 16
DC = 32
NTILE = 16
RLO, RHI = 611, 2047
EXP_SCALE = float(1.0 / np.sqrt(128.0))
EXP_BIAS = -40.0
EPS = 1e-5


def build_program(stage=3, dbg=0):
    nc = bass.Bass("TRN2", target_bir_lowering=False)

    x_pc = [nc.dram_tensor(f"x{i}", [512, D], F32, kind="ExternalInput").ap()
            for i in range(4)]
    xh_in = nc.dram_tensor("xh", [T, 2048], F32, kind="ExternalInput").ap()
    wft = nc.dram_tensor("wft", [OC, 128, DC * 128], F32, kind="ExternalInput").ap()
    wqt = nc.dram_tensor("wqt", [NH, 128, DC * 128], F32, kind="ExternalInput").ap()
    wot = nc.dram_tensor("wot", [OC, 128, DC * 128], F32, kind="ExternalInput").ap()
    wkt_in = nc.dram_tensor("wkt", [128, DC * 128], F32, kind="ExternalInput").ap()
    wvt_in = nc.dram_tensor("wvt", [128, DC * 128], F32, kind="ExternalInput").ap()
    act_in = nc.dram_tensor("acat", [128, DC * 128], F32, kind="ExternalInput").ap()
    bct_in = nc.dram_tensor("bcat", [128, OC * 128], F32, kind="ExternalInput").ap()
    rwt_in = nc.dram_tensor("rwt", [128, DC * 8], F32, kind="ExternalInput").ap()
    bfh_in = nc.dram_tensor("bfh", [128, OC], F32, kind="ExternalInput").ap()
    wqb_in = nc.dram_tensor("wqb", [128, NH], F32, kind="ExternalInput").ap()
    wob_in = nc.dram_tensor("wob", [128, OC], F32, kind="ExternalInput").ap()
    wkb_in = nc.dram_tensor("wkb", [128, 1], F32, kind="ExternalInput").ap()
    wvb_in = nc.dram_tensor("wvb", [1, 128], F32, kind="ExternalInput").ap()
    rb_in = nc.dram_tensor("rb", [1, 8], F32, kind="ExternalInput").ap()
    idf_in = nc.dram_tensor("idf", [128, 128], F32, kind="ExternalInput").ap()
    idb_in = nc.dram_tensor("idb", [128, 128], BF16, kind="ExternalInput").ap()
    dmask_in = nc.dram_tensor("dmask", [128, 128], BF16, kind="ExternalInput").ap()
    ctl_in = nc.dram_tensor("ctl", [1, 16], I32, kind="ExternalInput").ap()

    out = nc.dram_tensor("out", [T, 2048], F32, kind="ExternalOutput").ap()
    dbg_out = None
    if dbg in (1, 2):
        dbg_out = nc.dram_tensor("dbg", [T, D], F32, kind="ExternalOutput").ap()
    if dbg == 10:
        dbg_y = nc.dram_tensor("dbg_y", [128, 65536], F32, kind="ExternalOutput").ap()
        dbg_r = nc.dram_tensor("dbg_r", [128, 34], F32, kind="ExternalOutput").ap()
        dbg_t = nc.dram_tensor("dbg_t", [128, T], F32, kind="ExternalOutput").ap()
    if stage == 14:
        dbg_y2 = nc.dram_tensor("dbg_y2", [128, 65536], F32,
                                kind="ExternalOutput").ap()
    if stage in (15, 145, 146):
        dbg_k = nc.dram_tensor("dbg_k", [128, T], F32, kind="ExternalOutput").ap()
        dbg_v = nc.dram_tensor("dbg_v", [128, NTILE * 129], F32,
                               kind="ExternalOutput").ap()
        dbg_q = None
        if stage == 15:
            dbg_q = nc.dram_tensor("dbg_q", [NH * 128, T], F32,
                                   kind="ExternalOutput").ap()
    if stage in (16, 161, 162, 164):
        dbg_a = nc.dram_tensor("dbg_a", [NH * 128, T], F32,
                               kind="ExternalOutput").ap()

    x1s = [nc.dram_tensor(f"x1s{i}", [512, D], F32, kind="Internal",
                          addr_space="Shared") for i in range(4)]
    x2s = [nc.dram_tensor(f"x2s{i}", [512, D], F32, kind="Internal",
                          addr_space="Shared") for i in range(4)]
    av_sh = nc.dram_tensor("av_sh", [D, T], BF16, kind="Internal", addr_space="Shared")
    flag_sh = nc.dram_tensor("flag_sh", [6, 8], I32, kind="Internal",
                             addr_space="Shared")
    qt_dram = nc.dram_tensor("qt_dram", [NH * 128, T], BF16, kind="Internal")
    x1own = nc.dram_tensor("x1own", [T, 2048], F32, kind="Internal")
    x2own = nc.dram_tensor("x2own", [T, 2048], F32, kind="Internal")
    av_own = nc.dram_tensor("av_own", [NH * 128, T], BF16, kind="Internal")

    es = contextlib.ExitStack()
    sbt = lambda name, shape, dt: es.enter_context(nc.sbuf_tensor(name, shape, dt))
    pst = lambda name, shape, dt: es.enter_context(nc.psum_tensor(name, shape, dt))

    ybig = sbt("ybig", [128, 65536], BF16)
    big2 = sbt("big2", [128, 8192], BF16)
    xt = [sbt(f"xt{i}", [128, 2048], F32) for i in range(2)]
    U = sbt("ubuf", [128, 16400], BF16)
    acts = U[:, 0:4096]
    bct = U[:, 4096:6144]
    ttb = U[:, 6144:8192]
    yt = U[:, 8192:12288]
    wkts = U[:, 0:4096]
    wvts = U[:, 4096:8192]
    avth = U[:, 0:2048]
    ktb = U[:, 8192:10240]
    vtb = U[:, 10240:12304]
    qbuf = [U[:, 12304:14352], U[:, 14352:16400]]
    fstage = [sbt(f"fst{i}", [128, 512], F32) for i in range(2)]
    rbuf = [sbt(f"rbuf{i}", [128, 512], F32) for i in range(2)]
    avst = [sbt(f"avst{i}", [128, 128], BF16) for i in range(2)]
    rwts = sbt("rwts", [128, DC * 8], BF16)
    idf = sbt("idfs", [128, 128], F32)
    idb = sbt("idbs", [128, 128], BF16)
    dmask = sbt("dmk", [128, 128], BF16)
    bfh = sbt("bfhs", [128, OC], F32)
    bfh05 = sbt("bfh05", [128, OC], F32)
    wqb = sbt("wqbs", [128, NH], F32)
    wob = sbt("wobs", [128, OC], F32)
    wkb = sbt("wkbs", [128, 1], F32)
    wvb = sbt("wvbs", [1, 128], BF16)
    onesc = sbt("onesc", [1, 128], BF16)
    onef = sbt("onef", [1, 1], F32)
    rbs = sbt("rbs", [1, 8], F32)
    ctls = sbt("ctls", [1, 16], I32)
    fls = sbt("fls", [1, 16], I32)
    rsum = sbt("rsum", [128, DC], F32)
    q1b = sbt("q1b", [128, DC], BF16)
    lgs = sbt("lgs", [1, 8], F32)
    mxs = sbt("mxs", [1, 1], F32)
    ers = sbt("ers", [1, 8], F32)
    ess = sbt("ess", [1, 1], F32)
    esr = sbt("esr", [1, 1], F32)
    rout = sbt("rout", [1, 8], F32)
    rvrow = sbt("rvrow", [1, 128], F32)
    rvec = sbt("rvec", [128, 1], F32)
    sm = [sbt(f"sm{i}", [128, 1], F32) for i in range(2)]
    sq = [sbt(f"sq{i}", [128, 1], F32) for i in range(2)]
    msum = sbt("msum", [128, 1], F32)
    msq = sbt("msq", [128, 1], F32)
    mmean = sbt("mmean", [128, 1], F32)
    mvar = sbt("mvar", [128, 1], F32)
    msd = sbt("msd", [128, 1], F32)
    epsb = sbt("epsb", [128, 1], F32)
    expb = sbt("expb", [128, 1], F32)
    mrstd = sbt("mrstd", [128, 1], F32)
    linv = sbt("linv", [128, 1], F32)

    psA = [pst(f"psA{i}", [128, 512], F32) for i in range(2)]
    psT = [pst(f"psT{i}", [128, 1024], BF16) for i in range(2)]
    psF = [pst(f"psF{i}", [128, 512], F32) for i in range(2)]
    psR = pst("psR", [128, 8], F32)
    psV = pst("psV", [128, 8], F32)

    names = ["sConst", "sCtl", "sSmallW", "sPre",
             "sXL", "sSq", "sSum", "sStats", "sSd", "sNrm", "sPsTe", "sTpos",
             "sRs", "sQ1", "sLg", "sLg2", "sEr", "sRv1", "sRv2", "sTT", "sTp",
             "sWL", "sMM", "sFS", "sTRf", "sS2", "sOutD", "sRL", "sBulk",
             "sXdma", "sXDone", "sVch",
             "sKQV", "sQw", "sQpf", "sHq",
             "sSc", "sExp", "sMask", "sAV", "sAVe", "sTrA", "sAvtE", "sAvW",
             "sAVF"]
    sems = {}
    for n in names:
        sems[n] = es.enter_context(nc.semaphore(n))
    C = dict.fromkeys(names, 0)

    ytr = ybig[:].rearrange("p (c t) -> p c t", t=T)

    with es, nc.Block() as block:
        g0 = nc.gpsimd
        regs = {}
        for rn in ["par", "r_off", "r_av", "r_fl", "r_pn", "r_cnt", "r_f0",
                   "r_f1", "r_t", "r_pr", "r_nd", "r_d",
                   "r_tc0", "r_tc1", "r_tc2", "r_tc3"]:
            regs[rn] = es.enter_context(g0.register(rn))
        par, r_off, r_av = regs["par"], regs["r_off"], regs["r_av"]
        r_fl, r_pn, r_cnt = regs["r_fl"], regs["r_pn"], regs["r_cnt"]
        r_f0, r_f1, r_t = regs["r_f0"], regs["r_f1"], regs["r_t"]
        r_pr, r_nd, r_d = regs["r_pr"], regs["r_nd"], regs["r_d"]
        r_tc = [regs[f"r_tc{i}"] for i in range(4)]

        @block.sync
        def _(sp):
            sp.dma_start(out=ctls[:], in_=ctl_in).then_inc(sems["sCtl"], 16)
            for dst, src in [(idf, idf_in), (idb, idb_in), (dmask, dmask_in),
                             (bfh, bfh_in), (wqb, wqb_in), (wob, wob_in),
                             (wkb, wkb_in), (rbs, rb_in)]:
                sp.dma_start(out=dst[:], in_=src).then_inc(sems["sConst"], 16)
        C["sCtl"] = 16
        C["sConst"] = 8 * 16

        @block.gpsimd
        def _(g):
            for dst, srcap in [(rwts[:], rwt_in), (wvb[:], wvb_in),
                               (acts, act_in), (bct, bct_in)]:
                g.dma_start(out=dst, in_=srcap).then_inc(sems["sSmallW"], 16)
            g.wait_ge(sems["sCtl"], 16)
            g.reg_load(par, ctls[0:1, 0:1])
            g.reg_alu(r_off, par, 2048, ALU.mult)
            g.reg_alu(r_av, r_off, T, ALU.mult)
            g.reg_mov(r_cnt, 0)
        C["sSmallW"] = 4 * 16

        @block.vector
        def _(v):
            v.wait_ge(sems["sConst"], C["sConst"])
            v.tensor_scalar(out=bfh05[:], in0=bfh[:], scalar1=0.5, scalar2=None,
                            op0=ALU.mult)
            v.memset(onef[:], 1.0)
            v.memset(onesc[:], 1.0)
            v.memset(epsb[:], EPS)
            v.memset(expb[:], EXP_BIAS)


        def flag_and_spin(g, event):
            g.reg_alu(r_fl, par, 8, ALU.mult)
            if event:
                g.reg_alu(r_fl, r_fl, event * 16, ALU.add)
            g.reg_alu(r_cnt, r_cnt, 16, ALU.add)
            g.dma_start(out=AP(flag_sh, r_fl, [[8, 1], [1, 1]]),
                        in_=ctls[0:1, 3 + event:4 + event]
                        ).then_inc(sems["sXdma"], 16)
            g.reg_load(r_pn, ctls[0:1, 6 + event:7 + event])

            def poll():
                g.reg_alu(r_cnt, r_cnt, 16, ALU.add)
                g.dma_start(out=fls[:],
                            in_=AP(flag_sh, event * 16, [[16, 1], [1, 16]])
                            ).then_inc(sems["sXdma"], 16)
                g.wait_ge(sems["sXdma"], r_cnt)
                g.reg_load(r_f0, fls[0:1, 0:1])
                g.reg_load(r_f1, fls[0:1, 8:9])
                g.reg_alu(r_t, r_f0, r_f1, ALU.subtract)
                g.reg_alu(r_t, r_t, par, ALU.mult)
                g.reg_alu(r_pr, r_f1, r_t, ALU.add)
                g.reg_alu(r_nd, r_pr, r_pn, ALU.not_equal)

            poll()
            with g.While(r_nd):
                poll()
            g.nop().then_inc(sems["sXDone"], 1)
            C["sXDone"] += 1

        def emit_ln(src_slice, xgate):
            nt = NTILE
            base = {k: C[k] for k in ("sXL", "sSq", "sSum", "sStats", "sSd",
                                      "sNrm", "sPsTe", "sTpos")}

            @block.sync
            def _(sp):
                if xgate:
                    sp.wait_ge(sems["sXDone"], xgate)
                for tt in range(nt):
                    for ch in range(2):
                        idx = tt * 2 + ch
                        if idx >= 2:
                            sp.wait_ge(sems["sNrm"], base["sNrm"] + idx - 1)
                        sp.dma_start(out=xt[idx % 2][:], in_=src_slice(tt, ch)
                                     ).then_inc(sems["sXL"], 16)
            C["sXL"] += nt * 2 * 16

            @block.scalar
            def _(s):
                for tt in range(nt):
                    for ch in range(2):
                        idx = tt * 2 + ch
                        s.wait_ge(sems["sXL"], base["sXL"] + (idx + 1) * 16)
                        if tt >= 1 and ch == 0:
                            s.wait_ge(sems["sStats"], base["sStats"] + tt)
                            s.wait_ge(sems["sTpos"], base["sTpos"] + tt * 4)
                        s.activation(out=yt[:, ch * 2048:(ch + 1) * 2048],
                                     in_=xt[idx % 2][:], func=AF.Square,
                                     accum_out=sq[ch][:]).then_inc(sems["sSq"], 1)
                    s.wait_ge(sems["sStats"], base["sStats"] + tt + 1)
                    s.activation(out=msd[:], in_=mvar[:], func=AF.Sqrt,
                                 bias=epsb[:], scale=1.0).then_inc(sems["sSd"], 1)
                    for b8 in range(4):
                        bidx = tt * 4 + b8
                        s.wait_ge(sems["sTpos"], base["sTpos"] + bidx + 1)
                        s.copy(out=ytr[:, b8 * 8:(b8 + 1) * 8,
                                       tt * 128:(tt + 1) * 128],
                               in_=psT[bidx % 2][:].rearrange(
                                   "p (c t) -> p c t", t=128)
                               ).then_inc(sems["sPsTe"], 1)
            C["sSq"] += nt * 2
            C["sSd"] += nt
            C["sPsTe"] += nt * 4

            @block.vector
            def _(v):
                for tt in range(nt):
                    for ch in range(2):
                        idx = tt * 2 + ch
                        v.wait_ge(sems["sXL"], base["sXL"] + (idx + 1) * 16)
                        ins = v.reduce_sum(sm[ch][:], xt[idx % 2][:], axis=AX.X)
                    ins.then_inc(sems["sVch"], 1)
                    C["sVch"] += 1
                    v.wait_ge(sems["sVch"], C["sVch"])
                    v.wait_ge(sems["sSq"], base["sSq"] + tt * 2 + 2)
                    v.tensor_add(msum[:], sm[0][:], sm[1][:])
                    v.tensor_add(msq[:], sq[0][:], sq[1][:]
                                 ).then_inc(sems["sVch"], 1)
                    C["sVch"] += 1
                    v.wait_ge(sems["sVch"], C["sVch"])
                    v.tensor_scalar(out=mmean[:], in0=msum[:], scalar1=1.0 / D,
                                    scalar2=None, op0=ALU.mult)
                    v.tensor_scalar(out=msq[:], in0=msq[:], scalar1=1.0 / D,
                                    scalar2=None, op0=ALU.mult
                                    ).then_inc(sems["sVch"], 1)
                    C["sVch"] += 1
                    v.wait_ge(sems["sVch"], C["sVch"])
                    v.tensor_mul(mvar[:], mmean[:], mmean[:]
                                 ).then_inc(sems["sVch"], 1)
                    C["sVch"] += 1
                    v.wait_ge(sems["sVch"], C["sVch"])
                    v.tensor_sub(mvar[:], msq[:], mvar[:]
                                 ).then_inc(sems["sStats"], 1)
                    v.wait_ge(sems["sSd"], base["sSd"] + tt + 1)
                    v.reciprocal(mrstd[:], msd[:]).then_inc(sems["sVch"], 1)
                    C["sVch"] += 1
                    v.wait_ge(sems["sVch"], C["sVch"])
                    for ch in range(2):
                        idx = tt * 2 + ch
                        v.tensor_scalar(out=yt[:, ch * 2048:(ch + 1) * 2048],
                                        in0=xt[idx % 2][:], scalar1=mmean[:],
                                        scalar2=mrstd[:], op0=ALU.subtract,
                                        op1=ALU.mult).then_inc(sems["sNrm"], 1)
            C["sStats"] += nt
            C["sNrm"] += nt * 2

            @block.tensor
            def _(t):
                for tt in range(nt):
                    t.wait_ge(sems["sNrm"], base["sNrm"] + tt * 2 + 2)
                    for dc in range(DC):
                        bidx = tt * 4 + dc // 8
                        if dc % 8 == 0 and bidx >= 2:
                            t.wait_ge(sems["sPsTe"], base["sPsTe"] + bidx - 1)
                        ins = t.transpose(
                            psT[bidx % 2][:, (dc % 8) * 128:(dc % 8 + 1) * 128],
                            yt[:, dc * 128:(dc + 1) * 128], idb[:])
                        if dc % 8 == 7:
                            ins.then_inc(sems["sTpos"], 1)
            C["sTpos"] += nt * 4

        def emit_lora_a():
            b_mm, b_tt = C["sMM"], C["sTT"]

            @block.tensor
            def _(t):
                t.wait_ge(sems["sPsTe"], C["sPsTe"])
                t.wait_ge(sems["sSmallW"], C["sSmallW"])
                for tc in range(4):
                    if tc >= 2:
                        t.wait_ge(sems["sTT"], b_tt + tc - 1)
                    for dc in range(DC):
                        ins = t.matmul(psA[tc % 2][:],
                                       acts[:, dc * 128:(dc + 1) * 128],
                                       ytr[:, dc, tc * 512:(tc + 1) * 512],
                                       start=(dc == 0), stop=(dc == DC - 1))
                        if dc == DC - 1:
                            ins.then_inc(sems["sMM"], 1)
            C["sMM"] += 4

            @block.vector
            def _(v):
                for tc in range(4):
                    v.wait_ge(sems["sMM"], b_mm + tc + 1)
                    v.tensor_copy(ttb[:, tc * 512:(tc + 1) * 512], psA[tc % 2][:]
                                  ).then_inc(sems["sTT"], 1)
            C["sTT"] += 4

        def emit_routing():
            @block.vector
            def _(v):
                v.wait_ge(sems["sPsTe"], C["sPsTe"])
                for dc in range(DC):
                    ins = v.reduce_sum(rsum[:, dc:dc + 1], ytr[:, dc, RLO:RHI],
                                       axis=AX.X)
                ins.then_inc(sems["sVch"], 1)
                C["sVch"] += 1
                v.wait_ge(sems["sVch"], C["sVch"])
                v.tensor_scalar(out=q1b[:], in0=rsum[:],
                                scalar1=1.0 / (RHI - RLO), scalar2=None,
                                op0=ALU.mult).then_inc(sems["sQ1"], 1)
            C["sQ1"] += 1

            @block.tensor
            def _(t):
                t.wait_ge(sems["sQ1"], C["sQ1"])
                for dc in range(DC):
                    ins = t.matmul(psR[0:1, 0:8], q1b[:, dc:dc + 1],
                                   rwts[:, dc * 8:(dc + 1) * 8],
                                   start=(dc == 0), stop=(dc == DC - 1))
                    if dc == DC - 1:
                        ins.then_inc(sems["sLg"], 1)
            C["sLg"] += 1

            @block.vector
            def _(v):
                v.wait_ge(sems["sLg"], C["sLg"])
                v.tensor_add(lgs[:], psR[0:1, 0:8], rbs[:]
                             ).then_inc(sems["sVch"], 1)
                C["sVch"] += 1
                v.wait_ge(sems["sVch"], C["sVch"])
                v.reduce_max(mxs[:], lgs[:], axis=AX.X, negate=True
                             ).then_inc(sems["sLg2"], 1)
            C["sLg2"] += 1

            @block.scalar
            def _(s):
                s.wait_ge(sems["sLg2"], C["sLg2"])
                s.activation(out=ers[:], in_=lgs[:], func=AF.Exp, bias=mxs[:],
                             scale=1.0, accum_out=ess[:]).then_inc(sems["sEr"], 1)
            C["sEr"] += 1

            @block.vector
            def _(v):
                v.wait_ge(sems["sEr"], C["sEr"])
                v.reciprocal(esr[:], ess[:]).then_inc(sems["sVch"], 1)
                C["sVch"] += 1
                v.wait_ge(sems["sVch"], C["sVch"])
                v.tensor_scalar(out=rout[:], in0=ers[:], scalar1=esr[:],
                                scalar2=None, op0=ALU.mult
                                ).then_inc(sems["sVch"], 1)
                C["sVch"] += 1
                v.wait_ge(sems["sVch"], C["sVch"])
                v.tensor_copy(rvrow[:].rearrange("p (e r) -> p e r", r=16),
                              rout[0:1, 0:8].unsqueeze(-1).to_broadcast(
                                  [1, 8, 16])).then_inc(sems["sRv1"], 1)
            C["sRv1"] += 1

            @block.tensor
            def _(t):
                t.wait_ge(sems["sRv1"], C["sRv1"])
                t.matmul(psV[:, 0:1], rvrow[:], onef[:], start=True, stop=True
                         ).then_inc(sems["sRv2"], 1)
            C["sRv2"] += 1

            @block.vector
            def _(v):
                v.wait_ge(sems["sRv2"], C["sRv2"])
                v.wait_ge(sems["sTT"], C["sTT"])
                v.tensor_copy(rvec[:], psV[:, 0:1]).then_inc(sems["sVch"], 1)
                C["sVch"] += 1
                v.wait_ge(sems["sVch"], C["sVch"])
                v.tensor_scalar(out=ttb, in0=ttb, scalar1=rvec[:],
                                scalar2=2.0, op0=ALU.mult, op1=ALU.mult
                                ).then_inc(sems["sTp"], 1)
            C["sTp"] += 1

        def emit_proj(wstream, bias, scale, resid_src, dest, with_lora,
                      resid_gate=None):
            NB = OC * 4
            b = {k: C[k] for k in ("sWL", "sMM", "sFS", "sTRf", "sS2", "sOutD",
                                   "sRL")}
            wslot = [big2[:, 0:4096], big2[:, 4096:8192]]
            dstride = 2048

            @block.sync
            def _(sp):
                if resid_gate is not None:
                    sp.wait_ge(sems[resid_gate[0]], resid_gate[1])
                for g_ in range(NB):
                    oc, tc = g_ // 4, g_ % 4
                    if g_ >= 2:
                        sp.wait_ge(sems["sOutD"], b["sOutD"] + (g_ - 1) * 16)
                    sp.dma_start(
                        out=rbuf[g_ % 2][:].rearrange("p (i f) -> p i f", f=128),
                        in_=AP(resid_src, tc * 512 * dstride + oc * 128,
                               [[dstride, 128], [128 * dstride, 4], [1, 128]])
                        ).then_inc(sems["sRL"], 16)
            C["sRL"] += NB * 16

            def _tr(t, j):
                t.wait_ge(sems["sFS"], b["sFS"] + j + 1)
                if j >= 2:
                    t.wait_ge(sems["sS2"], b["sS2"] + j - 1)
                for i in range(4):
                    ins = t.transpose(psF[j % 2][:, i * 128:(i + 1) * 128],
                                      fstage[j % 2][:, i * 128:(i + 1) * 128],
                                      idf[:])
                    if i == 3:
                        ins.then_inc(sems["sTRf"], 1)

            @block.tensor
            def _(t):
                if with_lora:
                    t.wait_ge(sems["sTp"], C["sTp"])
                for g_ in range(NB):
                    oc, tc = g_ // 4, g_ % 4
                    if tc == 0:
                        t.wait_ge(sems["sWL"], b["sWL"] + (oc + 1) * 16)
                    if g_ >= 2:
                        t.wait_ge(sems["sFS"], b["sFS"] + g_ - 1)
                    for dc in range(DC):
                        ins = t.matmul(psA[g_ % 2][:],
                                       wslot[oc % 2][:, dc * 128:(dc + 1) * 128],
                                       ytr[:, dc, tc * 512:(tc + 1) * 512],
                                       start=(dc == 0),
                                       stop=(dc == DC - 1 and not with_lora))
                        if dc == DC - 1 and not with_lora:
                            ins.then_inc(sems["sMM"], 1)
                    if with_lora:
                        t.matmul(psA[g_ % 2][:], bct[:, oc * 128:(oc + 1) * 128],
                                 ttb[:, tc * 512:(tc + 1) * 512],
                                 start=False, stop=True).then_inc(sems["sMM"], 1)
                    if g_ >= 1:
                        _tr(t, g_ - 1)
                _tr(t, NB - 1)
            C["sMM"] += NB
            C["sTRf"] += NB

            def _s2(v, j):
                v.wait_ge(sems["sTRf"], b["sTRf"] + j + 1)
                v.wait_ge(sems["sRL"], b["sRL"] + (j + 1) * 16)
                v.tensor_add(rbuf[j % 2][:], psF[j % 2][:], rbuf[j % 2][:]
                             ).then_inc(sems["sS2"], 1)

            @block.vector
            def _(v):
                for g_ in range(NB):
                    oc = g_ // 4
                    v.wait_ge(sems["sMM"], b["sMM"] + g_ + 1)
                    if g_ >= 2:
                        v.wait_ge(sems["sTRf"], b["sTRf"] + g_ - 1)
                    v.tensor_scalar(out=fstage[g_ % 2][:], in0=psA[g_ % 2][:],
                                    scalar1=scale, scalar2=bias[:, oc:oc + 1],
                                    op0=ALU.mult, op1=ALU.add
                                    ).then_inc(sems["sFS"], 1)
                    if g_ >= 1:
                        _s2(v, g_ - 1)
                _s2(v, NB - 1)
            C["sFS"] += NB
            C["sS2"] += NB

            @block.gpsimd
            def _(g):
                for oc in range(2):
                    g.dma_start(out=wslot[oc][:], in_=wstream[oc]
                                ).then_inc(sems["sWL"], 16)
                for g_ in range(NB):
                    oc, tc = g_ // 4, g_ % 4
                    g.wait_ge(sems["sS2"], b["sS2"] + g_ + 1)
                    dst = AP(dest, tc * 512 * dstride + oc * 128,
                             [[dstride, 128], [128 * dstride, 4], [1, 128]])
                    g.dma_start(out=dst,
                                in_=rbuf[g_ % 2][:].rearrange(
                                    "p (i f) -> p i f", f=128)
                                ).then_inc(sems["sOutD"], 16)
                    if tc == 3 and oc + 2 < OC:
                        g.wait_ge(sems["sMM"], b["sMM"] + (oc + 1) * 4)
                        g.dma_start(out=wslot[oc % 2][:], in_=wstream[oc + 2]
                                    ).then_inc(sems["sWL"], 16)
            C["sWL"] += OC * 16
            C["sOutD"] += NB * 16

        emit_ln(lambda tt, ch: x_pc[tt // 4][(tt % 4) * 128:
                                             (tt % 4 + 1) * 128,
                                             ch * 2048:(ch + 1) * 2048], 0)
        emit_lora_a()
        emit_routing()
        emit_proj([wft[oc] for oc in range(OC)], bfh05, 0.5, xh_in.tensor,
                  x1own, True)
        mm_after_ffn1 = C["sMM"]

        if dbg == 10:
            @block.gpsimd
            def _(g):
                g.wait_ge(sems["sTp"], C["sTp"])
                g.dma_start(out=dbg_y, in_=ybig[:]).then_inc(sems["sOutD"], 16)
                g.dma_start(out=dbg_r[:, 0:32], in_=rsum[:]).then_inc(sems["sOutD"], 16)
                g.dma_start(out=dbg_r[:, 32:33], in_=rvec[:],
                            allow_slow_non_contiguous=True
                            ).then_inc(sems["sOutD"], 16)
                g.dma_start(out=dbg_r[:, 33:34], in_=mrstd[:],
                            allow_slow_non_contiguous=True
                            ).then_inc(sems["sOutD"], 16)
                g.dma_start(out=dbg_t, in_=ttb).then_inc(sems["sOutD"], 16)
            C["sOutD"] += 5 * 16

        @block.gpsimd
        def _(g):
            g.wait_ge(sems["sOutD"], C["sOutD"])
            for i in range(4):
                g.dma_start(out=AP(x1s[i], r_off, [[D, 512], [1, 2048]]),
                            in_=x1own.ap()[i * 512:(i + 1) * 512, :]
                            ).then_inc(sems["sBulk"], 16)
            g.wait_ge(sems["sBulk"], C["sBulk"] + 64)
            flag_and_spin(g, 0)
        C["sBulk"] += 64

        if dbg == 1:
            @block.gpsimd
            def _(g):
                g.wait_ge(sems["sXDone"], C["sXDone"])
                for i in range(4):
                    g.dma_start(out=dbg_out[i * 512:(i + 1) * 512, :],
                                in_=x1s[i].ap()).then_inc(sems["sOutD"], 16)
            C["sOutD"] += 4 * 16

        if stage == 1:
            @block.gpsimd
            def _(g):
                g.wait_ge(sems["sXDone"], C["sXDone"])
                for i in range(4):
                    g.dma_start(out=out[i * 512:(i + 1) * 512, :],
                                in_=AP(x1s[i], 0, [[D, 512], [1, 2048]])
                                ).then_inc(sems["sOutD"], 16)
                g.wait_ge(sems["sOutD"], C["sOutD"] + 64)
            C["sOutD"] += 64
            return nc

        x1done = C["sXDone"]

        emit_ln(lambda tt, ch: x1s[tt // 4].ap()[(tt % 4) * 128:
                                                 (tt % 4 + 1) * 128,
                                                 ch * 2048:(ch + 1) * 2048],
                x1done)

        if stage == 14:
            @block.gpsimd
            def _(g):
                g.wait_ge(sems["sPsTe"], C["sPsTe"])
                g.dma_start(out=dbg_y2, in_=ybig[:]).then_inc(sems["sOutD"], 16)
                g.wait_ge(sems["sOutD"], C["sOutD"] + 16)
            C["sOutD"] += 16
            return nc

        @block.gpsimd
        def _(g):
            g.wait_ge(sems["sMM"], mm_after_ffn1)
            g.dma_start(out=wkts, in_=wkt_in).then_inc(sems["sSmallW"], 16)
            g.dma_start(out=wvts, in_=wvt_in).then_inc(sems["sSmallW"], 16)
        C["sSmallW"] += 32
        kv_loaded = C["sSmallW"]

        b_mm, b_kqv, b_qw, b_wl = C["sMM"], C["sKQV"], C["sQw"], C["sWL"]
        wslot2 = [big2[:, 0:4096], big2[:, 4096:8192]]

        nq = 0 if stage == 145 else NH
        nqw = 0 if stage in (145, 146) else nq

        @block.gpsimd
        def _(g):
            for h in range(min(nq, 2)):
                g.dma_start(out=wslot2[h % 2][:], in_=wqt[h]
                            ).then_inc(sems["sWL"], 16)
            for h in range(nq):
                if nqw:
                    g.wait_ge(sems["sKQV"], b_kqv + 20 + (h + 1) * 4)
                    g.dma_start(out=qt_dram.ap()[h * 128:(h + 1) * 128, :],
                                in_=qbuf[h % 2]).then_inc(sems["sQw"], 16)
                if h + 2 < nq:
                    g.wait_ge(sems["sMM"], b_mm + 20 + (h + 1) * 4)
                    g.dma_start(out=wslot2[h % 2][:], in_=wqt[h + 2]
                                ).then_inc(sems["sWL"], 16)
        C["sWL"] += nq * 16

        @block.tensor
        def _(t):
            t.wait_ge(sems["sPsTe"], C["sPsTe"])
            t.wait_ge(sems["sSmallW"], kv_loaded)
            gi = 0
            for tc in range(4):
                if gi >= 2:
                    t.wait_ge(sems["sKQV"], b_kqv + gi - 1)
                for dc in range(DC):
                    ins = t.matmul(psA[gi % 2][:],
                                   wkts[:, dc * 128:(dc + 1) * 128],
                                   ytr[:, dc, tc * 512:(tc + 1) * 512],
                                   start=(dc == 0), stop=(dc == DC - 1))
                    if dc == DC - 1:
                        ins.then_inc(sems["sMM"], 1)
                gi += 1
            for j in range(NTILE):
                if j >= 2:
                    t.wait_ge(sems["sKQV"], b_kqv + gi - 1)
                for dc in range(DC):
                    t.matmul(psF[j % 2][:, 0:128],
                             ytr[:, dc, j * 128:(j + 1) * 128],
                             wvts[:, dc * 128:(dc + 1) * 128],
                             start=(dc == 0), stop=False)
                t.matmul(psF[j % 2][:, 0:128], onesc[:], wvb[:],
                         start=False, stop=True).then_inc(sems["sMM"], 1)
                gi += 1
            for h in range(nq):
                t.wait_ge(sems["sWL"], b_wl + (h + 1) * 16)
                for tc in range(4):
                    if gi >= 2:
                        t.wait_ge(sems["sKQV"], b_kqv + gi - 1)
                    for dc in range(DC):
                        ins = t.matmul(psA[gi % 2][:],
                                       wslot2[h % 2][:, dc * 128:(dc + 1) * 128],
                                       ytr[:, dc, tc * 512:(tc + 1) * 512],
                                       start=(dc == 0), stop=(dc == DC - 1))
                        if dc == DC - 1:
                            ins.then_inc(sems["sMM"], 1)
                    gi += 1
        C["sMM"] += 4 + NTILE + nq * 4

        @block.vector
        def _(v):
            v.wait_ge(sems["sPsTe"], C["sPsTe"])
            for i in range(NTILE):
                v.memset(vtb[:, i * 129 + 128:i * 129 + 129], 1.0)
            gi = 0
            for tc in range(4):
                v.wait_ge(sems["sMM"], b_mm + gi + 1)
                v.tensor_scalar(out=ktb[:, tc * 512:(tc + 1) * 512],
                                in0=psA[gi % 2][:], scalar1=wkb[:],
                                scalar2=None, op0=ALU.add
                                ).then_inc(sems["sKQV"], 1)
                gi += 1
            for j in range(NTILE):
                v.wait_ge(sems["sMM"], b_mm + gi + 1)
                v.tensor_copy(vtb[:, j * 129:j * 129 + 128],
                              psF[j % 2][:, 0:128]).then_inc(sems["sKQV"], 1)
                gi += 1
            for h in range(nq):
                if h >= 2 and nqw:
                    v.wait_ge(sems["sQw"], b_qw + (h - 1) * 16)
                for tc in range(4):
                    v.wait_ge(sems["sMM"], b_mm + gi + 1)
                    v.tensor_scalar(out=qbuf[h % 2][:, tc * 512:(tc + 1) * 512],
                                    in0=psA[gi % 2][:], scalar1=wqb[:, h:h + 1],
                                    scalar2=None, op0=ALU.add
                                    ).then_inc(sems["sKQV"], 1)
                    gi += 1
        C["sKQV"] += 4 + NTILE + nq * 4
        C["sQw"] += nqw * 16

        if stage in (15, 145, 146):
            @block.gpsimd
            def _(g):
                g.wait_ge(sems["sQw"], C["sQw"])
                g.wait_ge(sems["sKQV"], C["sKQV"])
                g.dma_start(out=dbg_k, in_=ktb).then_inc(sems["sOutD"], 16)
                g.dma_start(out=dbg_v, in_=vtb).then_inc(sems["sOutD"], 16)
                nd = 32
                if stage == 146:
                    g.wait_ge(sems["sXdma"], 0)
                if stage == 15:
                    g.dma_start(out=dbg_q, in_=qt_dram.ap()
                                ).then_inc(sems["sOutD"], 16)
                    nd = 48
                g.wait_ge(sems["sOutD"], C["sOutD"] + nd)
            C["sOutD"] += 48 if stage == 15 else 32
            return nc

        NHA = {161: 1, 162: 2, 164: 4}.get(stage, NH)
        b_sc, b_exp, b_mask, b_av, b_ave = (C["sSc"], C["sExp"], C["sMask"],
                                            C["sAV"], C["sAVe"])
        b_tra, b_avte, b_avw, b_qpf = C["sTrA"], C["sAvtE"], C["sAvW"], C["sQpf"]
        qw_all = C["sQw"]
        ptb = big2

        @block.sync
        def _(sp):
            sp.wait_ge(sems["sQw"], qw_all)
            for h in range(NHA):
                if h >= 2:
                    sp.wait_ge(sems["sHq"], C["sHq"] + h - 1)
                sp.dma_start(out=qbuf[h % 2],
                             in_=qt_dram.ap()[h * 128:(h + 1) * 128, :]
                             ).then_inc(sems["sQpf"], 16)
        C["sQpf"] += NH * 16

        def _trav(t, a):
            t.wait_ge(sems["sAVe"], b_ave + a + 1)
            bank = (a // 8) % 2
            if a % 8 == 0 and a >= 16:
                t.wait_ge(sems["sAvtE"], b_avte + a // 8 - 1)
            ins = t.transpose(psT[bank][:, (a % 8) * 128:(a % 8 + 1) * 128],
                              avst[a % 2][:], idb[:])
            ins.then_inc(sems["sTrA"], 1)

        @block.tensor
        def _(t):
            nonlocal_sc = 0
            av_a = 0
            for h in range(NHA):
                t.wait_ge(sems["sQpf"], b_qpf + (h + 1) * 16)
                for qc in range(4):
                    nkt = 4 * qc + 4
                    for kt in range(nkt):
                        si = nonlocal_sc
                        if si >= 2:
                            t.wait_ge(sems["sExp"], b_exp + si - 1)
                        ins = t.matmul(psA[si % 2][:],
                                       ktb[:, kt * 128:(kt + 1) * 128],
                                       qbuf[h % 2][:, qc * 512:(qc + 1) * 512],
                                       start=True, stop=True)
                        ins.then_inc(sems["sSc"], 1)
                        if qc == 3 and kt == nkt - 1:
                            t.nop().then_inc(sems["sHq"], 1)
                        nonlocal_sc += 1
                    exp_done_base = b_exp + (nonlocal_sc - nkt)
                    for j in range(4):
                        nk = 4 * qc + j + 1
                        a = av_a
                        if a >= 2:
                            t.wait_ge(sems["sAVe"], b_ave + a - 1)
                        t.wait_ge(sems["sExp"], exp_done_base + nk)
                        t.wait_ge(sems["sMask"], b_mask + (h * 4 + qc) * 4 + j + 1)
                        for kt in range(nk):
                            ins = t.matmul(
                                psF[a % 2][:, 0:129],
                                ptb[:, kt * 512 + j * 128:kt * 512 + (j + 1) * 128],
                                vtb[:, kt * 129:(kt + 1) * 129],
                                start=(kt == 0), stop=(kt == nk - 1))
                            if kt == nk - 1:
                                ins.then_inc(sems["sAV"], 1)
                        if a % 16 != 0:
                            _trav(t, a - 1)
                        av_a += 1
                _trav(t, av_a - 1)

        C["sSc"] += NHA * 40
        C["sHq"] += NHA
        C["sAV"] += NHA * 16
        C["sTrA"] += NHA * 16

        @block.scalar
        def _(s):
            si = 0
            ai = 0
            for h in range(NHA):
                if h >= 1:
                    s.wait_ge(sems["sAvW"], b_avw + h * 16)
                for qc in range(4):
                    nkt = 4 * qc + 4
                    if h + qc > 0:
                        s.wait_ge(sems["sAV"], b_av + (h * 4 + qc - 1) * 4 + 4)
                    for kt in range(nkt):
                        s.wait_ge(sems["sSc"], b_sc + si + 1)
                        s.activation(out=ptb[:, kt * 512:(kt + 1) * 512],
                                     in_=psA[si % 2][:], func=AF.Exp,
                                     bias=expb[:], scale=EXP_SCALE
                                     ).then_inc(sems["sExp"], 1)
                        si += 1
                for b8 in range(2):
                    s.wait_ge(sems["sTrA"], b_tra + h * 16 + (b8 + 1) * 8)
                    s.copy(out=avth.rearrange("p (a t) -> p a t", t=128)
                           [:, b8 * 8:(b8 + 1) * 8, :],
                           in_=psT[(h * 2 + b8) % 2][:].rearrange(
                               "p (c t) -> p c t", t=128)
                           ).then_inc(sems["sAvtE"], 1)
        C["sExp"] += NHA * 40
        C["sAvtE"] += NHA * 2

        @block.gpsimd
        def _(g):
            ei = 0
            for h in range(NHA):
                for qc in range(4):
                    nkt = 4 * qc + 4
                    for j in range(4):
                        kt = 4 * qc + j
                        g.wait_ge(sems["sExp"], b_exp + ei + kt + 1)
                        g.tensor_mul(
                            ptb[:, kt * 512 + j * 128:kt * 512 + (j + 1) * 128],
                            ptb[:, kt * 512 + j * 128:kt * 512 + (j + 1) * 128],
                            dmask[:]).then_inc(sems["sMask"], 1)
                    ei += nkt
                g.wait_ge(sems["sAvtE"], b_avte + (h + 1) * 2)
                g.dma_start(out=av_own.ap()[h * 128:(h + 1) * 128, :],
                            in_=avth).then_inc(sems["sAvW"], 16)
        C["sMask"] += NHA * 16
        C["sAvW"] += NHA * 16

        @block.vector
        def _(v):
            for a in range(NHA * 16):
                v.wait_ge(sems["sAV"], b_av + a + 1)
                if a >= 2:
                    v.wait_ge(sems["sTrA"], b_tra + a - 1)
                v.reciprocal(linv[:], psF[a % 2][:, 128:129]
                             ).then_inc(sems["sVch"], 1)
                C["sVch"] += 1
                v.wait_ge(sems["sVch"], C["sVch"])
                v.tensor_scalar(out=avst[a % 2][:], in0=psF[a % 2][:, 0:128],
                                scalar1=linv[:], scalar2=None, op0=ALU.mult
                                ).then_inc(sems["sAVe"], 1)
        C["sAVe"] += NHA * 16

        if stage in (16, 161, 162, 164):
            @block.gpsimd
            def _(g):
                g.wait_ge(sems["sAvW"], C["sAvW"])
                g.dma_start(out=dbg_a, in_=av_own.ap()).then_inc(sems["sOutD"], 16)
                g.wait_ge(sems["sOutD"], C["sOutD"] + 16)
            C["sOutD"] += 16
            return nc

        @block.gpsimd
        def _(g):
            g.wait_ge(sems["sAvW"], C["sAvW"])
            g.dma_start(out=AP(av_sh, r_av, [[T, NH * 128], [1, T]]),
                        in_=av_own.ap()).then_inc(sems["sBulk"], 16)
            g.wait_ge(sems["sBulk"], C["sBulk"] + 16)
            flag_and_spin(g, 1)
        C["sBulk"] += 16

        avdone = C["sXDone"]

        @block.sync
        def _(sp):
            sp.wait_ge(sems["sXDone"], avdone)
            sp.dma_start(out=ybig[:].rearrange("p (c t) -> p c t", t=T),
                         in_=av_sh.ap().rearrange("(c p) t -> p c t", p=128)
                         ).then_inc(sems["sAVF"], 16)
        C["sAVF"] += 16

        @block.tensor
        def _(t):
            t.wait_ge(sems["sAVF"], C["sAVF"])
        emit_proj([wot[oc] for oc in range(OC)], wob, 1.0, x1own, x2own,
                  False)

        @block.gpsimd
        def _(g):
            g.wait_ge(sems["sOutD"], C["sOutD"])
            for i in range(4):
                g.dma_start(out=AP(x2s[i], r_off, [[D, 512], [1, 2048]]),
                            in_=x2own.ap()[i * 512:(i + 1) * 512, :]
                            ).then_inc(sems["sBulk"], 16)
            g.wait_ge(sems["sBulk"], C["sBulk"] + 64)
            flag_and_spin(g, 2)
        C["sBulk"] += 64

        if dbg == 2:
            @block.gpsimd
            def _(g):
                g.wait_ge(sems["sXDone"], C["sXDone"])
                for i in range(4):
                    g.dma_start(out=dbg_out[i * 512:(i + 1) * 512, :],
                                in_=x2s[i].ap()).then_inc(sems["sOutD"], 16)
            C["sOutD"] += 64

        if stage == 2:
            @block.gpsimd
            def _(g):
                g.wait_ge(sems["sXDone"], C["sXDone"])
                for i in range(4):
                    g.dma_start(out=out[i * 512:(i + 1) * 512, :],
                                in_=AP(x2s[i], 0, [[D, 512], [1, 2048]])
                                ).then_inc(sems["sOutD"], 16)
                g.wait_ge(sems["sOutD"], C["sOutD"] + 64)
            C["sOutD"] += 64
            return nc

        x2done = C["sXDone"]

        @block.gpsimd
        def _(g):
            g.wait_ge(sems["sAvW"], C["sAvW"])
            g.dma_start(out=acts, in_=act_in).then_inc(sems["sSmallW"], 16)
            g.dma_start(out=bct, in_=bct_in).then_inc(sems["sSmallW"], 16)
        C["sSmallW"] += 32

        emit_ln(lambda tt, ch: x2s[tt // 4].ap()[(tt % 4) * 128:
                                                  (tt % 4 + 1) * 128,
                                                  ch * 2048:(ch + 1) * 2048],
                x2done)
        emit_lora_a()
        emit_routing()
        emit_proj([wft[oc] for oc in range(OC)], bfh05, 0.5, x2own,
                  out.tensor, True)

        @block.gpsimd
        def _(g):
            g.wait_ge(sems["sOutD"], C["sOutD"])

    return nc



def _tile_lhsT(w_half):
    nblk = w_half.shape[0] // 128
    a = w_half.reshape(nblk, 128, DC, 128).transpose(0, 3, 2, 1)
    return np.ascontiguousarray(a.reshape(nblk, 128, DC * 128))


def _tile_contract(w):
    f = w.shape[0]
    a = w.reshape(f, DC, 128).transpose(2, 1, 0)
    return np.ascontiguousarray(a.reshape(128, DC * f))


def _prep_inputs(inputs):
    x = np.asarray(inputs["L0_attn_x"], dtype=np.float32)
    wq = np.asarray(inputs["L0_attn_wq_w"], dtype=np.float32)
    wqb_f = np.asarray(inputs["L0_attn_wq_b"], dtype=np.float32)
    wk = np.asarray(inputs["L0_attn_wk_w"], dtype=np.float32)
    wkb_f = np.asarray(inputs["L0_attn_wk_b"], dtype=np.float32)
    wv = np.asarray(inputs["L0_attn_wv_w"], dtype=np.float32)
    wvb_f = np.asarray(inputs["L0_attn_wv_b"], dtype=np.float32)
    wo = np.asarray(inputs["L0_attn_wo_w"], dtype=np.float32)
    wob_f = np.asarray(inputs["L0_attn_wo_b"], dtype=np.float32)
    W = np.asarray(inputs["L0_ffn_W"], dtype=np.float32)
    fb = np.asarray(inputs["L0_ffn_b"], dtype=np.float32)
    A = np.asarray(inputs["L0_ffn_A"], dtype=np.float32)
    Bm = np.asarray(inputs["L0_ffn_B"], dtype=np.float32)
    rw = np.asarray(inputs["L0_ffn_router_W"], dtype=np.float32)
    rb = np.asarray(inputs["L0_ffn_router_b"], dtype=np.float32)

    A_cat = A.reshape(128, D)
    B_cat = Bm.transpose(1, 0, 2).reshape(D, 128)

    wk_t = _tile_lhsT(wk)[0]
    wv_t = _tile_lhsT(wv)[0]
    a_t = _tile_lhsT(A_cat)[0]
    rw_t = _tile_contract(rw)

    idf = np.eye(128, dtype=np.float32)
    idb = np.eye(128, dtype=ml_dtypes.bfloat16)
    dm = (np.arange(128)[:, None] <= np.arange(128)[None, :]).astype(
        ml_dtypes.bfloat16)

    rng = np.random.default_rng()
    in_maps = []
    for c in range(8):
        b = c // 2
        p = c % 2
        sl = slice(p * 2048, (p + 1) * 2048)
        hsl = slice(p * NH * 128, (p + 1) * NH * 128)
        m = {
            "wft": _tile_lhsT(W[sl]),
            "wqt": _tile_lhsT(wq[hsl]),
            "wot": _tile_lhsT(wo[sl]),
            "wkt": wk_t, "wvt": wv_t, "acat": a_t,
            "bcat": np.ascontiguousarray(
                B_cat[sl].reshape(OC, 128, 128).transpose(2, 0, 1)
                .reshape(128, OC * 128)),
            "rwt": rw_t,
            "bfh": np.ascontiguousarray(fb[sl].reshape(OC, 128).T),
            "wqb": np.ascontiguousarray(wqb_f[hsl].reshape(NH, 128).T),
            "wob": np.ascontiguousarray(wob_f[sl].reshape(OC, 128).T),
            "wkb": wkb_f.reshape(128, 1),
            "wvb": wvb_f.reshape(1, 128),
            "rb": rb.reshape(1, 8),
            "idf": idf, "idb": idb, "dmask": dm,
        }
        for i in range(4):
            m[f"x{i}"] = np.ascontiguousarray(x[b][i * 512:(i + 1) * 512])
        m["xh"] = np.ascontiguousarray(x[b][:, p * 2048:(p + 1) * 2048])
        in_maps.append(m)
    for b in range(4):
        n0 = int(rng.integers(1 << 20, 1 << 30))
        n1 = int(rng.integers(1 << 20, 1 << 30))
        ctl0 = np.zeros((1, 16), dtype=np.int32)
        ctl1 = np.zeros((1, 16), dtype=np.int32)
        ctl0[0, 0] = 0
        ctl1[0, 0] = 1
        for k in range(3):
            ctl0[0, 3 + k] = n0 + k + 1
            ctl0[0, 6 + k] = n1 + k + 1
            ctl1[0, 3 + k] = n1 + k + 1
            ctl1[0, 6 + k] = n0 + k + 1
        in_maps[2 * b]["ctl"] = ctl0
        in_maps[2 * b + 1]["ctl"] = ctl1
    return in_maps


def kernel(**inputs):
    in_maps = _prep_inputs(inputs)
    nc = build_program(stage=3, dbg=0)
    res = run_bass_kernel_spmd(nc, in_maps, core_ids=list(range(8)))
    outp = np.empty((4, T, D), dtype=np.float32)
    for c in range(8):
        b, p = c // 2, c % 2
        outp[b][:, p * 2048:(p + 1) * 2048] = res.results[c]["out"]
    return outp
